# version 03
# baseline (speedup 1.0000x reference)
import math
from contextlib import ExitStack

import numpy as np
import ml_dtypes
import concourse.bass as bass
import concourse.mybir as mybir
from concourse.bass_utils import run_bass_kernel_spmd

F32 = mybir.dt.float32
BF16 = mybir.dt.bfloat16
I32 = mybir.dt.int32
AF = mybir.ActivationFunctionType
ALU = mybir.AluOpType
AX = mybir.AxisListType
bfdt = ml_dtypes.bfloat16

NCORES = 8
P = 128
D = 4096
S = 4096
B = 2
CHUNK = 1024
NT = CHUNK // P
FC = D // P
HD = 128
NQH, NKV = 32, 8
GQ = NQH // NKV
WIN = 1024
NCLUST = 64
TEMP, THRESH, EPS = 0.1, 0.01, 1e-5
SMS = 1.0 / math.sqrt(HD)
NEXP = 8

ENGS = ("sync", "gpsimd", "vector", "scalar", "tensor")


class Ctr:
    def __init__(self, sem):
        self.sem = sem
        self.n = 0

    def inc(self, k=1):
        self.n += k
        return self.n


class Promise:
    def __init__(self):
        self.val = None


class PB:

    def __init__(self, nc):
        self.nc = nc
        self.ops = {k: [] for k in ENGS}

    def on(self, eng, fn):
        self.ops[eng].append(fn)

    def wait(self, eng, ctr_or_sem, val):
        sem = ctr_or_sem.sem if isinstance(ctr_or_sem, Ctr) else ctr_or_sem
        if isinstance(val, Promise):
            self.on(eng, lambda e, sem=sem, val=val: e.wait_ge(sem, val.val))
        else:
            self.on(eng, lambda e, sem=sem, val=val: e.wait_ge(sem, val))

    def emit(self, blk):
        for name in ENGS:
            fns = self.ops[name]

            def runner(e, fns=fns):
                for f in fns:
                    f(e)
            getattr(blk, name)(runner)


def ln_tile(p, vv, sc, x_f32, xn_bf, sq_scr, stats, tcol):
    scol = stats[:, tcol:tcol + 1]
    mcol = stats[:, 8 + tcol:8 + tcol + 1]
    rcol = stats[:, 16 + tcol:16 + tcol + 1]
    p.on("scalar", lambda e: e.activation(out=sq_scr[:, :], in_=x_f32[:, :],
                                          func=AF.Square, accum_out=scol).then_inc(sc.sem, 1))
    c_sq = sc.inc()
    p.on("vector", lambda e: e.reduce_sum(mcol, x_f32[:, :], axis=AX.X).then_inc(vv.sem, 1))
    c_sum = vv.inc()
    p.wait("vector", vv, c_sum)
    p.wait("vector", sc, c_sq)
    p.on("vector", lambda e: e.tensor_scalar_mul(mcol, mcol, -1.0 / D).then_inc(vv.sem, 1))
    p.wait("vector", vv, vv.inc())
    p.on("vector", lambda e: e.tensor_tensor(rcol, mcol, mcol, ALU.mult).then_inc(vv.sem, 1))
    p.wait("vector", vv, vv.inc())
    p.on("vector", lambda e: e.tensor_scalar(out=rcol, in0=scol, scalar1=1.0 / D,
                                             scalar2=None, op0=ALU.mult).then_inc(vv.sem, 1))
    p.wait("vector", vv, vv.inc())
    tmp = stats[:, 24 + tcol:24 + tcol + 1]
    p.on("vector", lambda e: e.tensor_tensor(tmp, mcol, mcol, ALU.mult).then_inc(vv.sem, 1))
    p.wait("vector", vv, vv.inc())
    p.on("vector", lambda e: e.tensor_tensor(rcol, rcol, tmp, ALU.subtract).then_inc(vv.sem, 1))
    p.wait("vector", vv, vv.inc())
    p.on("vector", lambda e: e.tensor_scalar(out=rcol, in0=rcol, scalar1=EPS,
                                             scalar2=None, op0=ALU.add).then_inc(vv.sem, 1))
    c_var = vv.inc()
    p.wait("scalar", vv, c_var)
    p.on("scalar", lambda e: e.activation(out=rcol, in_=rcol, func=AF.Sqrt).then_inc(sc.sem, 1))
    p.wait("vector", sc, sc.inc())
    p.on("vector", lambda e: e.reciprocal(rcol, rcol).then_inc(vv.sem, 1))
    p.wait("vector", vv, vv.inc())
    p.on("vector", lambda e: e.tensor_scalar(out=xn_bf[:, :], in0=x_f32[:, :], scalar1=mcol,
                                             scalar2=rcol, op0=ALU.add,
                                             op1=ALU.mult).then_inc(vv.sem, 1))
    return vv.inc()


def build_launch1():
    nc = bass.Bass("TRN2", target_bir_lowering=False)
    x0 = nc.dram_tensor("x0", [CHUNK, D], F32, kind="ExternalInput").ap()
    ct = nc.dram_tensor("ct", [D, NCLUST], F32, kind="ExternalInput").ap()
    wk = nc.dram_tensor("wk", [D, NKV * HD], F32, kind="ExternalInput").ap()
    wv = nc.dram_tensor("wv", [D, NKV * HD], F32, kind="ExternalInput").ap()
    cosb = nc.dram_tensor("cosb", [HD // 2, CHUNK], F32, kind="ExternalInput").ap()
    sinb = nc.dram_tensor("sinb", [HD // 2, CHUNK], F32, kind="ExternalInput").ap()
    ident = nc.dram_tensor("ident", [P, P], BF16, kind="ExternalInput").ap()

    counts_o = nc.dram_tensor("counts", [1, NCLUST], F32, kind="ExternalOutput").ap()
    qsel_o = nc.dram_tensor("qsel", [CHUNK, 1], F32, kind="ExternalOutput").ap()
    xn_o = nc.dram_tensor("xn0", [D, CHUNK], BF16, kind="ExternalOutput").ap()
    k_o = nc.dram_tensor("k0", [NKV * HD, CHUNK], BF16, kind="ExternalOutput").ap()
    v_o = nc.dram_tensor("v0", [CHUNK, NKV * HD], BF16, kind="ExternalOutput").ap()

    es = ExitStack()
    with es:
        sb = lambda name, shape, dt=F32: es.enter_context(nc.sbuf_tensor(name, shape, dt))
        psum = lambda name, shape, dt=F32: es.enter_context(nc.psum_tensor(name, shape, dt))
        sem = lambda name: Ctr(es.enter_context(nc.semaphore(name)))

        xn_fm = sb("xn_fm", [P, FC * CHUNK], BF16)
        xt = [sb(f"xt{i}", [P, D]) for i in range(2)]
        xnt = [sb(f"xnt{i}", [P, D], BF16) for i in range(2)]
        sq_scr = sb("sq_scr", [P, D])
        stats = sb("stats", [P, 32])
        cts = sb("cts", [P, FC * NCLUST], BF16)
        wsl = [sb(f"wsl{i}", [P, FC * P], BF16) for i in range(2)]
        cos_s = sb("cos_s", [HD // 2, CHUNK])
        sin_s = sb("sin_s", [HD // 2, CHUNK])
        idb = sb("idb", [P, P], BF16)
        ones_col = sb("ones_col", [P, 1], BF16)
        dotsf = sb("dotsf", [P, NCLUST])
        indb = sb("indb", [P, NCLUST], BF16)
        selc = sb("selc", [P, 4])
        rp1 = sb("rp1", [P, 512])
        rp2 = sb("rp2", [P, 512])
        kbf = sb("kbf", [P, CHUNK], BF16)
        vbf = sb("vbf", [P, P], BF16)
        cnt_row = sb("cnt_row", [1, NCLUST])
        ps_tr = psum("ps_tr", [P, 512], BF16)
        ps_a = psum("ps_a", [P, 512])
        ps_b = psum("ps_b", [P, 512])
        ps_c = psum("ps_c", [P, NCLUST])

        ld = sem("s_ld")
        cst = sem("s_cst")
        ctl = sem("s_ctl")
        gld = sem("s_gld")
        vv = sem("s_v")
        sc = sem("s_sc")
        tt = sem("s_t")
        st = sem("s_st")

        with nc.Block() as blk:
            p = PB(nc)
            xnf_r = xn_fm.rearrange("p (fc t) -> p fc t", t=CHUNK)
            x_tiled = x0.rearrange("(n p) d -> n p d", p=P)
            xn_od = xn_o.rearrange("(fc p) t -> p fc t", p=P)
            ct_r = ct.rearrange("(fc p) c -> p fc c", p=P)
            cts_r = cts.rearrange("p (fc c) -> p fc c", c=NCLUST)
            wk_r = wk.rearrange("(fc p) o -> p fc o", p=P)
            wv_r = wv.rearrange("(fc p) o -> p fc o", p=P)
            wsl_r = [w.rearrange("p (fc o) -> p fc o", o=P) for w in wsl]
            k_od = k_o.rearrange("(dc p) t -> dc p t", p=P)
            v_od = v_o.rearrange("(n p) o -> n p o", p=P)

            p.on("sync", lambda e: e.dma_start(out=idb[:, :], in_=ident[:, :]).then_inc(cst.sem, 16))
            p.on("sync", lambda e: e.dma_start(out=cos_s[:, :], in_=cosb[:, :]).then_inc(cst.sem, 16))
            p.on("sync", lambda e: e.dma_start(out=sin_s[:, :], in_=sinb[:, :]).then_inc(cst.sem, 16))
            cst.inc(48)
            CONSTS = cst.n
            for h in range(4):
                sl = slice(h * 8, (h + 1) * 8)
                p.on("gpsimd", lambda e, sl=sl: e.dma_start(
                    out=cts_r[:, sl, :], in_=ct_r[:, sl, :]).then_inc(ctl.sem, 16))
                ctl.inc(16)
            CT_DONE = ctl.n
            p.on("vector", lambda e: e.memset(ones_col[:, :], 1.0).then_inc(vv.sem, 1))
            ONES = vv.inc()

            x_ld = []
            tr_done = []
            ln_done = []
            for t in range(NT):
                buf, nbuf = xt[t % 2], xnt[t % 2]
                if t >= 2:
                    p.wait("sync", tt, tr_done[t - 2])
                p.on("sync", lambda e, t=t, buf=buf: e.dma_start(
                    out=buf[:, :], in_=x_tiled[t, :, :]).then_inc(ld.sem, 16))
                x_ld.append(ld.inc(16))
                p.wait("scalar", ld, x_ld[t])
                p.wait("vector", ld, x_ld[t])
                if t >= 2:
                    p.wait("vector", tt, tr_done[t - 2])
                ln_done.append(ln_tile(p, vv, sc, buf, nbuf, sq_scr, stats, t))
                p.wait("tensor", vv, ln_done[t])
                if t == 0:
                    p.wait("tensor", cst, CONSTS)
                for q in range(FC // 4):
                    for j in range(4):
                        fc = q * 4 + j
                        p.on("tensor", lambda e, nbuf=nbuf, fc=fc, j=j: e.transpose(
                            ps_tr[:, j * P:(j + 1) * P], nbuf[:, fc * P:(fc + 1) * P],
                            idb[:, :]).then_inc(tt.sem, 1) if j == 3 else e.transpose(
                            ps_tr[:, j * P:(j + 1) * P], nbuf[:, fc * P:(fc + 1) * P], idb[:, :]))
                    qt = tt.inc(1)
                    p.wait("vector", tt, qt)

                    def _evac(e, q=q, t=t):
                        for j in range(4):
                            fc = q * 4 + j
                            e.tensor_copy(out=xnf_r[:, fc, t * P:(t + 1) * P],
                                          in_=ps_tr[:, j * P:(j + 1) * P])
                        e.engine_nop().then_inc(vv.sem, 1)
                    p.on("vector", _evac)
                    ev = vv.inc()
                    p.wait("tensor", vv, ev)
                tr_done.append(tt.n)
                evn = vv.n
                p.wait("sync", vv, evn)
                for h in range(4):
                    sl = slice(h * 8, (h + 1) * 8)
                    p.on("sync", lambda e, t=t, sl=sl: e.dma_start(
                        out=xn_od[:, sl, t * P:(t + 1) * P],
                        in_=xnf_r[:, sl, t * P:(t + 1) * P]).then_inc(st.sem, 16))
                    st.inc(16)
            XN_ALL = vv.n

            p.wait("tensor", ctl, CT_DONE)
            p.wait("tensor", vv, XN_ALL)
            p.wait("tensor", vv, ONES)
            CNT_MM = Promise()
            prev_qok = None
            for t in range(NT):
                for fc in range(FC):
                    last = fc == FC - 1
                    p.on("tensor", lambda e, fc=fc, t=t, last=last: (
                        e.matmul(ps_c[0:P, :], xnf_r[:, fc, t * P:(t + 1) * P],
                                 cts_r[:, fc, :], start=(fc == 0), stop=last
                                 ).then_inc(tt.sem, 1) if last else
                        e.matmul(ps_c[0:P, :], xnf_r[:, fc, t * P:(t + 1) * P],
                                 cts_r[:, fc, :], start=(fc == 0), stop=last)))
                mmd = tt.inc()
                p.wait("scalar", tt, mmd)
                if prev_qok is not None:
                    p.wait("scalar", vv, prev_qok)
                p.on("scalar", lambda e: e.activation(
                    out=dotsf[:, :], in_=ps_c[0:P, :], func=AF.Exp,
                    scale=1.0 / TEMP).then_inc(sc.sem, 1))
                e1 = sc.inc()
                p.wait("tensor", sc, e1)
                p.wait("vector", sc, e1)
                p.on("vector", lambda e: e.reduce_sum(selc[:, 0:1], dotsf[:, :],
                                                      axis=AX.X).then_inc(vv.sem, 1))
                p.wait("vector", vv, vv.inc())
                p.on("vector", lambda e: e.reciprocal(selc[:, 0:1], selc[:, 0:1]).then_inc(vv.sem, 1))
                p.wait("vector", vv, vv.inc())
                p.on("vector", lambda e: e.tensor_scalar_mul(dotsf[:, :], dotsf[:, :],
                                                             selc[:, 0:1]).then_inc(vv.sem, 1))
                p.wait("vector", vv, vv.inc())
                p.on("vector", lambda e: e.tensor_scalar(
                    out=indb[:, :], in0=dotsf[:, :], scalar1=THRESH, scalar2=None,
                    op0=ALU.is_gt).then_inc(vv.sem, 1))
                ind_ok = vv.inc()
                p.on("vector", lambda e: e.reduce_max(selc[:, 1:2], dotsf[:, :],
                                                      axis=AX.X).then_inc(vv.sem, 1))
                p.wait("vector", vv, vv.inc())
                p.on("vector", lambda e: e.tensor_scalar(
                    out=selc[:, 1:2], in0=selc[:, 1:2], scalar1=THRESH, scalar2=None,
                    op0=ALU.is_gt).then_inc(vv.sem, 1))
                q_ok = vv.inc()
                prev_qok = q_ok
                p.wait("tensor", vv, ind_ok)
                p.on("tensor", lambda e, t=t: e.matmul(
                    ps_a[0:1, t * NCLUST:(t + 1) * NCLUST], ones_col[:, :], indb[:, :],
                    start=True, stop=True).then_inc(tt.sem, 1))
                cnt_t = tt.inc()
                if t == NT - 1:
                    CNT_MM.val = cnt_t
                p.wait("sync", vv, q_ok)
                p.on("sync", lambda e, t=t: e.dma_start(
                    out=qsel_o[t * P:(t + 1) * P, :], in_=selc[:, 1:2]).then_inc(st.sem, 16))
                st.inc(16)
                p.wait("vector", st, st.n)
            p.wait("vector", tt, CNT_MM)
            p.on("vector", lambda e: e.tensor_copy(out=cnt_row[:, :],
                                                   in_=ps_a[0:1, 0:NCLUST]).then_inc(vv.sem, 1))
            p.wait("vector", vv, vv.inc())
            for t2 in range(1, NT):
                p.on("vector", lambda e, t2=t2: e.tensor_add(
                    cnt_row[:, :], cnt_row[:, :],
                    ps_a[0:1, t2 * NCLUST:(t2 + 1) * NCLUST]).then_inc(vv.sem, 1))
                p.wait("vector", vv, vv.inc())
            cr = vv.n
            p.wait("sync", vv, cr)
            p.on("sync", lambda e: e.dma_start(out=counts_o[:, :],
                                               in_=cnt_row[:, :]).then_inc(st.sem, 16))
            st.inc(16)

            slab_done = []
            slab_consumed = [Promise() for _ in range(2 * NKV)]
            for i in range(2 * NKV):
                src_r = wk_r if i < NKV else wv_r
                dcol = (i if i < NKV else i - NKV) * P
                slr = wsl_r[i % 2]
                if i >= 2:
                    p.wait("gpsimd", tt, slab_consumed[i - 2])
                for h in range(4):
                    sl = slice(h * 8, (h + 1) * 8)
                    p.on("gpsimd", lambda e, slr=slr, src_r=src_r, dcol=dcol, sl=sl: e.dma_start(
                        out=slr[:, sl, :], in_=src_r[:, sl, dcol:dcol + P]).then_inc(gld.sem, 16))
                    gld.inc(16)
                slab_done.append(gld.n)

            p.wait("vector", cst, CONSTS)
            p.wait("tensor", vv, cr)
            for i in range(NKV):
                slr = wsl_r[i % 2]
                p.wait("tensor", gld, slab_done[min(i | 1, 2 * NKV - 1)])
                for tb in range(2):
                    pb = ps_a if tb == 0 else ps_b
                    for fc in range(FC):
                        last = fc == FC - 1
                        p.on("tensor", lambda e, slr=slr, fc=fc, tb=tb, pb=pb, last=last: (
                            e.matmul(pb[:, :], slr[:, fc, :],
                                     xnf_r[:, fc, tb * 512:(tb + 1) * 512],
                                     start=(fc == 0), stop=last).then_inc(tt.sem, 1)
                            if last else
                            e.matmul(pb[:, :], slr[:, fc, :],
                                     xnf_r[:, fc, tb * 512:(tb + 1) * 512],
                                     start=(fc == 0), stop=last)))
                    mmk = tt.inc()
                    p.wait("vector", tt, mmk)

                    def _rope(e, pb=pb, tb=tb):
                        H = HD // 2
                        cs = cos_s[:, tb * 512:(tb + 1) * 512]
                        sn = sin_s[:, tb * 512:(tb + 1) * 512]
                        e.tensor_tensor(rp1[0:H, :], pb[0:H, :], cs, ALU.mult)
                        e.tensor_tensor(rp2[0:H, :], pb[H:HD, :], sn, ALU.mult)
                        e.tensor_tensor(rp1[H:HD, :], pb[0:H, :], sn, ALU.mult)
                        e.tensor_tensor(rp2[H:HD, :], pb[H:HD, :], cs, ALU.mult)
                        e.engine_nop().then_inc(vv.sem, 1)
                    p.on("vector", _rope)
                    p.wait("vector", vv, vv.inc())

                    def _rope2(e, tb=tb):
                        H = HD // 2
                        kb = kbf[:, tb * 512:(tb + 1) * 512]
                        e.tensor_sub(kb[0:H, :], rp1[0:H, :], rp2[0:H, :])
                        e.tensor_add(kb[H:HD, :], rp1[H:HD, :], rp2[H:HD, :])
                        e.engine_nop().then_inc(vv.sem, 1)
                    p.on("vector", _rope2)
                    rope_done = vv.inc()
                    p.wait("tensor", vv, rope_done)
                kd = vv.n
                p.wait("sync", vv, kd)
                p.on("sync", lambda e, i=i: e.dma_start(
                    out=k_od[i, :, :], in_=kbf[:, :]).then_inc(st.sem, 16))
                st.inc(16)
                p.wait("vector", st, st.n)
                slab_consumed[i].val = tt.n

            for i in range(NKV):
                si = NKV + i
                slr = wsl_r[si % 2]
                p.wait("tensor", gld, slab_done[min(si | 1, 2 * NKV - 1)])
                for t in range(NT):
                    pb = ps_a if t % 2 == 0 else ps_b
                    for fc in range(FC):
                        last = fc == FC - 1
                        p.on("tensor", lambda e, slr=slr, fc=fc, t=t, pb=pb, last=last: (
                            e.matmul(pb[:, 0:P], xnf_r[:, fc, t * P:(t + 1) * P],
                                     slr[:, fc, :], start=(fc == 0), stop=last
                                     ).then_inc(tt.sem, 1) if last else
                            e.matmul(pb[:, 0:P], xnf_r[:, fc, t * P:(t + 1) * P],
                                     slr[:, fc, :], start=(fc == 0), stop=last)))
                    mmv = tt.inc()
                    p.wait("vector", tt, mmv)
                    p.on("vector", lambda e, pb=pb: e.tensor_copy(
                        out=vbf[:, :], in_=pb[:, 0:P]).then_inc(vv.sem, 1))
                    vd = vv.inc()
                    p.wait("tensor", vv, vd)
                    p.wait("sync", vv, vd)
                    p.on("sync", lambda e, t=t, i=i: e.dma_start(
                        out=v_od[t, :, i * P:(i + 1) * P], in_=vbf[:, :]).then_inc(st.sem, 16))
                    st.inc(16)
                    p.wait("vector", st, st.n)
                slab_consumed[si].val = tt.n

            p.wait("sync", st, st.n)
            p.emit(blk)
    return nc


def build_launch2():
    nc = bass.Bass("TRN2", target_bir_lowering=False)
    x0 = nc.dram_tensor("x0", [CHUNK, D], F32, kind="ExternalInput").ap()
    xn0 = nc.dram_tensor("xn0", [D, CHUNK], BF16, kind="ExternalInput").ap()
    kk = nc.dram_tensor("kk", [NKV * HD, 2 * CHUNK], BF16, kind="ExternalInput").ap()
    vvt = nc.dram_tensor("vvt", [2 * CHUNK, NKV * HD], BF16, kind="ExternalInput").ap()
    cnts = nc.dram_tensor("cnts", [NCORES, NCLUST], F32, kind="ExternalInput").ap()
    qsel = nc.dram_tensor("qsel", [CHUNK, 1], F32, kind="ExternalInput").ap()
    asg = nc.dram_tensor("asg", [1, D], I32, kind="ExternalInput").ap()
    wq = nc.dram_tensor("wq", [D, D], F32, kind="ExternalInput").ap()
    wo = nc.dram_tensor("wo", [D, D], F32, kind="ExternalInput").ap()
    w0t = nc.dram_tensor("w0t", [D, D], F32, kind="ExternalInput").ap()
    qkv1t = nc.dram_tensor("qkv1t", [D, 6144], F32, kind="ExternalInput").ap()
    bias0 = nc.dram_tensor("bias0", [1, D], F32, kind="ExternalInput").ap()
    cosb = nc.dram_tensor("cosb", [HD // 2, CHUNK], F32, kind="ExternalInput").ap()
    sinb = nc.dram_tensor("sinb", [HD // 2, CHUNK], F32, kind="ExternalInput").ap()
    mask0 = nc.dram_tensor("mask0", [P, 512], BF16, kind="ExternalInput").ap()
    mask8 = nc.dram_tensor("mask8", [P, 512], BF16, kind="ExternalInput").ap()
    maskc = nc.dram_tensor("maskc", [P, 512], BF16, kind="ExternalInput").ap()
    iota64 = nc.dram_tensor("iota64", [P, 1], F32, kind="ExternalInput").ap()
    ident = nc.dram_tensor("ident", [P, P], BF16, kind="ExternalInput").ap()

    x1_o = nc.dram_tensor("x1", [CHUNK, D], F32, kind="ExternalOutput").ap()
    xn1_o = nc.dram_tensor("xn1", [D, CHUNK], BF16, kind="ExternalOutput").ap()
    k1_o = nc.dram_tensor("k1", [CHUNK, NKV * HD], BF16, kind="ExternalOutput").ap()
    v1_o = nc.dram_tensor("v1", [CHUNK, NKV * HD], BF16, kind="ExternalOutput").ap()

    q0f = nc.dram_tensor("q0f", [NQH * HD, CHUNK], BF16, kind="Internal").ap()
    attnf = nc.dram_tensor("attnf", [NQH * HD, CHUNK], BF16, kind="Internal").ap()
    gdr = nc.dram_tensor("gdr", [CHUNK, D], BF16, kind="Internal").ap()

    es = ExitStack()
    with es:
        sb = lambda name, shape, dt=F32: es.enter_context(nc.sbuf_tensor(name, shape, dt))
        psum = lambda name, shape, dt=F32: es.enter_context(nc.psum_tensor(name, shape, dt))
        sem = lambda name: Ctr(es.enter_context(nc.semaphore(name)))

        xn_fm = sb("xn_fm", [P, FC * CHUNK], BF16)
        ws4 = sb("ws4", [P, FC * 512], BF16)
        qg = sb("qg", [P, GQ * CHUNK], BF16)
        kg = sb("kg", [P, 2 * CHUNK], BF16)
        vg = sb("vg", [P, 16 * P], BF16)
        probs = [sb(f"probs{i}", [P, 512], BF16) for i in range(2)]
        attn_sb = sb("attn_sb", [P, 512], BF16)
        m0 = sb("m0", [P, 512], BF16)
        m8 = sb("m8", [P, 512], BF16)
        mc = sb("mc", [P, 512], BF16)
        bias_bc = sb("bias_bc", [P, D], BF16)
        rowmask = sb("rowmask", [P, D], BF16)
        qsel_sb = sb("qsel_sb", [P, NT])
        acol = sb("acol", [P, FC * P], BF16)
        xt0 = sb("xt0", [P, D])
        gt = sb("gt", [P, 1024], BF16)
        xnt0 = sb("xnt0", [P, D], BF16)
        stats = sb("stats", [P, 32])
        cos_s = sb("cos_s", [HD // 2, CHUNK])
        sin_s = sb("sin_s", [HD // 2, CHUNK])
        rp1 = sb("rp1", [P, 512])
        rp2 = sb("rp2", [P, 512])
        kbf = sb("kbf", [P, CHUNK], BF16)
        vbf = sb("vbf", [P, P], BF16)
        idb = sb("idb", [P, P], BF16)
        ones1 = sb("ones1", [1, P], BF16)
        ones8 = sb("ones8", [NCORES, 1], BF16)
        onesP = sb("onesP", [P, P], BF16)
        iota_c = sb("iota_c", [P, 1])
        cnt_sb = sb("cnt_sb", [NCORES, NCLUST])
        cnt_bf = sb("cnt_bf", [NCORES, NCLUST], BF16)
        cs_col = sb("cs_col", [NCLUST, 1])
        cs_bf = sb("cs_bf", [NCLUST, 1], BF16)
        asg_sb = sb("asg_sb", [1, D], I32)
        brow_bf = sb("brow_bf", [1, D], BF16)
        ps_tr = psum("ps_tr", [P, 512], BF16)
        psA = psum("psA", [P, 512])
        psB = psum("psB", [P, 512])
        psC = psum("psC", [P, 512])
        psD = psum("psD", [P, 512])

        cst = sem("s_cst")
        sxn = sem("s_sxn")
        sq_ = sem("s_q")
        skv = sem("s_skv")
        sac = sem("s_sac")
        sxg = sem("s_sxg")
        sx1 = sem("s_sx1")
        gw = sem("s_gw")
        gbx = sem("s_gbx")
        vv = sem("s_v")
        sc = sem("s_sc")
        tt = sem("s_t")
        st = sem("s_st")

        with nc.Block() as blk:
            p = PB(nc)
            xnf_r = xn_fm.rearrange("p (fc t) -> p fc t", t=CHUNK)
            ws4_r = ws4.rearrange("p (fc o) -> p fc o", o=512)
            xn0_d = xn0.rearrange("(fc p) t -> p fc t", p=P)
            xn1_d = xn1_o.rearrange("(fc p) t -> p fc t", p=P)
            wq_r = wq.rearrange("(fc p) o -> p fc o", p=P)
            wo_r = wo.rearrange("(fc p) o -> p fc o", p=P)
            w0t_r = w0t.rearrange("(fc p) o -> p fc o", p=P)
            qkv_r = qkv1t.rearrange("(fc p) o -> p fc o", p=P)
            q0f_w = q0f.rearrange("(hc p) t -> hc p t", p=P)
            q0f_g = q0f.rearrange("(g h p) (n q) -> g p n h q", p=P, h=GQ, q=P)
            attnf_w = attnf.rearrange("(hc p) t -> hc p t", p=P)
            attnf_c = attnf.rearrange("(ec p) t -> p ec t", p=P)
            acol_r = acol.rearrange("p (ec t) -> p ec t", t=P)
            vvt_r = vvt.rearrange("(kb p) o -> p kb o", p=P)
            vg_r = vg.rearrange("p (kb q) -> p kb q", q=P)
            qg_r = qg.rearrange("p (n h q) -> p n h q", h=GQ, q=P)
            x1_w = x1_o.rearrange("(n p) d -> n p d", p=P)
            x0_r = x0.rearrange("(n p) d -> n p d", p=P)
            gdr_r = gdr.rearrange("(n p) d -> n p d", p=P)
            k1_w = k1_o.rearrange("(n p) o -> n p o", p=P)
            v1_w = v1_o.rearrange("(n p) o -> n p o", p=P)

            for apdst, apsrc in ((idb, ident), (cos_s, cosb), (sin_s, sinb),
                                 (m0, mask0), (m8, mask8), (mc, maskc),
                                 (iota_c, iota64), (cnt_sb, cnts), (asg_sb, asg)):
                p.on("sync", lambda e, a=apdst, s2=apsrc: e.dma_start(
                    out=a[:, :], in_=s2[:, :]).then_inc(cst.sem, 16))
                cst.inc(16)
            p.on("sync", lambda e: e.dma_start(
                out=qsel_sb[:, :], in_=qsel.rearrange("(t p) o -> p (t o)", p=P),
                allow_slow_non_contiguous=True).then_inc(cst.sem, 16))
            cst.inc(16)
            CONSTS = cst.n
            p.on("gpsimd", lambda e: e.dma_start(out=brow_bf[:, :],
                                                 in_=bias0[:, :]).then_inc(gbx.sem, 16))
            gbx.inc(16)
            BIAS_L = gbx.n
            p.on("vector", lambda e: e.memset(ones1[:, :], 1.0).then_inc(vv.sem, 1))
            p.on("vector", lambda e: e.memset(ones8[:, :], 1.0).then_inc(vv.sem, 1))
            p.on("vector", lambda e: e.memset(onesP[:, :], 1.0).then_inc(vv.sem, 1))
            vv.inc(3)
            ONES = vv.n

            for h in range(8):
                sl = slice(h * 4, (h + 1) * 4)
                p.on("sync", lambda e, sl=sl: e.dma_start(
                    out=xnf_r[:, sl, :], in_=xn0_d[:, sl, :]).then_inc(sxn.sem, 16))
                sxn.inc(16)
            XN_L = sxn.n

            p.wait("vector", cst, CONSTS)
            p.on("vector", lambda e: e.tensor_copy(out=cnt_bf[:, :], in_=cnt_sb[:, :]).then_inc(vv.sem, 1))
            p.wait("vector", gbx, BIAS_L)
            vv.inc(1)
            p.wait("vector", vv, vv.n)
            PRE_V = vv.n
            p.wait("tensor", vv, PRE_V)
            p.on("tensor", lambda e: e.matmul(psA[0:NCLUST, 0:1], cnt_bf[:, :], ones8[:, :],
                                              start=True, stop=True).then_inc(tt.sem, 1))
            t_cs = tt.inc()
            p.wait("vector", tt, t_cs)
            p.on("vector", lambda e: e.tensor_scalar(
                out=cs_col[:, :], in0=psA[0:NCLUST, 0:1], scalar1=0.5, scalar2=None,
                op0=ALU.is_gt).then_inc(vv.sem, 1))
            p.wait("vector", vv, vv.inc())
            p.on("vector", lambda e: e.tensor_copy(out=cs_bf[:, :], in_=cs_col[:, :]).then_inc(vv.sem, 1))
            CS_OK = vv.inc()
            p.wait("tensor", vv, CS_OK)
            p.wait("tensor", gbx, BIAS_L)
            for oc in range(8):
                osl = slice(oc * 512, (oc + 1) * 512)
                p.on("tensor", lambda e, osl=osl: e.matmul(
                    psB[:, :], ones1[:, :], brow_bf[:, osl], start=True,
                    stop=True).then_inc(tt.sem, 1))
                tb_ = tt.inc()
                p.wait("vector", tt, tb_)
                p.on("vector", lambda e, osl=osl: e.tensor_copy(
                    out=bias_bc[:, osl], in_=psB[:, :]).then_inc(vv.sem, 1))
                p.wait("tensor", vv, vv.inc())
            for oc in range(8):
                osl = slice(oc * 512, (oc + 1) * 512)
                p.on("vector", lambda e, osl=osl: e.tensor_copy(
                    out=probs[1][0:1, :], in_=asg_sb[:, osl]).then_inc(vv.sem, 1))
                a_ok = vv.inc()
                p.wait("tensor", vv, a_ok)
                p.on("tensor", lambda e: e.matmul(
                    psB[:, :], ones1[:, :], probs[1][0:1, :], start=True,
                    stop=True).then_inc(tt.sem, 1))
                t1 = tt.inc()
                p.wait("vector", tt, t1)
                p.on("vector", lambda e: e.tensor_scalar(
                    out=probs[0][:, :], in0=psB[:, :], scalar1=iota_c[:, :],
                    scalar2=None, op0=ALU.is_equal).then_inc(vv.sem, 1))
                v1_ = vv.inc()
                p.wait("tensor", vv, v1_)
                p.on("tensor", lambda e: e.matmul(
                    psB[0:1, :], cs_bf[:, :], probs[0][0:NCLUST, :], start=True,
                    stop=True).then_inc(tt.sem, 1))
                t2 = tt.inc()
                p.wait("vector", tt, t2)
                p.on("vector", lambda e, osl=osl: e.tensor_copy(
                    out=brow_bf[:, osl], in_=psB[0:1, :]).then_inc(vv.sem, 1))
                p.wait("tensor", vv, vv.inc())
            for oc in range(8):
                osl = slice(oc * 512, (oc + 1) * 512)
                p.on("tensor", lambda e, osl=osl: e.matmul(
                    psB[:, :], ones1[:, :], brow_bf[:, osl], start=True,
                    stop=True).then_inc(tt.sem, 1))
                t3 = tt.inc()
                p.wait("vector", tt, t3)
                p.on("vector", lambda e, osl=osl: e.tensor_copy(
                    out=rowmask[:, osl], in_=psB[:, :]).then_inc(vv.sem, 1))
                p.wait("tensor", vv, vv.inc())
            PRE_DONE = vv.n

            slab_tok = Promise()
            slab_tok.val = 0
            gw_marks = []

            def load_ws4(src_r, col0, width=512, after=None):
                if after is not None:
                    p.wait("gpsimd", tt, after)
                for h in range(8):
                    sl = slice(h * 4, (h + 1) * 4)
                    p.on("gpsimd", lambda e, sl=sl, src_r=src_r, col0=col0, width=width: e.dma_start(
                        out=ws4_r[:, sl, 0:width],
                        in_=src_r[:, sl, col0:col0 + width]).then_inc(gw.sem, 16))
                    gw.inc(16)
                gw_marks.append(gw.n)
                return gw.n

            prev_use = None
            for s in range(8):
                mark = load_ws4(wq_r, s * 512, after=prev_use)
                pr = Promise()
                prev_use = pr
                p.wait("tensor", gw, mark)
                if s == 0:
                    p.wait("tensor", sxn, XN_L)
                    p.wait("vector", cst, CONSTS)
                for hh in range(4):
                    for tb in range(2):
                        pb = psA if tb == 0 else psB
                        for fc in range(FC):
                            last = fc == FC - 1
                            p.on("tensor", lambda e, fc=fc, hh=hh, tb=tb, pb=pb, last=last: (
                                e.matmul(pb[:, :], ws4_r[:, fc, hh * P:(hh + 1) * P],
                                         xnf_r[:, fc, tb * 512:(tb + 1) * 512],
                                         start=(fc == 0), stop=last).then_inc(tt.sem, 1)
                                if last else
                                e.matmul(pb[:, :], ws4_r[:, fc, hh * P:(hh + 1) * P],
                                         xnf_r[:, fc, tb * 512:(tb + 1) * 512],
                                         start=(fc == 0), stop=last)))
                        mmq = tt.inc()
                        p.wait("vector", tt, mmq)

                        def _rope(e, pb=pb, tb=tb):
                            H2 = HD // 2
                            cs2 = cos_s[:, tb * 512:(tb + 1) * 512]
                            sn2 = sin_s[:, tb * 512:(tb + 1) * 512]
                            e.tensor_tensor(rp1[0:H2, :], pb[0:H2, :], cs2, ALU.mult)
                            e.tensor_tensor(rp2[0:H2, :], pb[H2:HD, :], sn2, ALU.mult)
                            e.tensor_tensor(rp1[H2:HD, :], pb[0:H2, :], sn2, ALU.mult)
                            e.tensor_tensor(rp2[H2:HD, :], pb[H2:HD, :], cs2, ALU.mult)
                            e.engine_nop().then_inc(vv.sem, 1)
                        p.on("vector", _rope)
                        p.wait("vector", vv, vv.inc())

                        def _rope2(e, tb=tb):
                            H2 = HD // 2
                            kb2 = kbf[:, tb * 512:(tb + 1) * 512]
                            e.tensor_sub(kb2[0:H2, :], rp1[0:H2, :], rp2[0:H2, :])
                            e.tensor_add(kb2[H2:HD, :], rp1[H2:HD, :], rp2[H2:HD, :])
                            e.engine_nop().then_inc(vv.sem, 1)
                        p.on("vector", _rope2)
                        rd = vv.inc()
                        p.wait("tensor", vv, rd)
                    qd = vv.n
                    p.wait("sync", vv, qd)
                    p.on("sync", lambda e, s=s, hh=hh: e.dma_start(
                        out=q0f_w[s * 4 + hh, :, :], in_=kbf[:, :]).then_inc(st.sem, 16))
                    st.inc(16)
                    p.wait("vector", st, st.n)
                pr.val = tt.n
            Q_DONE = st.n

            p.wait("sync", st, Q_DONE)
            attn_marks = []
            for g in range(NKV):
                p.on("sync", lambda e, g=g: e.dma_start(
                    out=qg_r[:, :, :, :], in_=q0f_g[g]).then_inc(sq_.sem, 16))
                sq_.inc(16)
                p.on("sync", lambda e, g=g: e.dma_start(
                    out=kg[:, :], in_=kk[g * P:(g + 1) * P, :]).then_inc(skv.sem, 16))
                skv.inc(16)
                p.on("sync", lambda e, g=g: e.dma_start(
                    out=vg_r[:, :, :], in_=vvt_r[:, :, g * P:(g + 1) * P]).then_inc(skv.sem, 16))
                skv.inc(16)
                p.wait("tensor", sq_, sq_.n)
                p.wait("tensor", skv, skv.n)
                if g == 0:
                    p.wait("tensor", vv, PRE_DONE)
                for n in range(NT):
                    ex_done = []
                    av_done = []
                    for r in range(9):
                        pb = psA if r % 2 == 0 else psB
                        prb = probs[r % 2]
                        kslc = (n + r) * P
                        if r >= 2:
                            p.wait("tensor", sc, ex_done[r - 2])
                        p.on("tensor", lambda e, pb=pb, kslc=kslc, n=n: e.matmul(
                            pb[:, :], kg[:, kslc:kslc + P],
                            qg[:, n * 512:(n + 1) * 512],
                            start=True, stop=True).then_inc(tt.sem, 1))
                        smm = tt.inc()
                        p.wait("scalar", tt, smm)
                        if r >= 2:
                            p.wait("scalar", tt, av_done[r - 2])
                        p.on("scalar", lambda e, pb=pb, prb=prb: e.activation(
                            out=prb[:, :], in_=pb[:, :], func=AF.Exp,
                            scale=SMS).then_inc(sc.sem, 1))
                        ex_done.append(sc.inc())
                        mask_t = None
                        if r == 0:
                            mask_t = m0
                        elif r == 8:
                            mask_t = m8
                        elif n + r < 8:
                            mask_t = mc
                        if mask_t is not None:
                            p.wait("vector", sc, ex_done[r])
                            p.on("vector", lambda e, prb=prb, mask_t=mask_t: e.tensor_tensor(
                                prb[:, :], prb[:, :], mask_t[:, :], ALU.mult).then_inc(vv.sem, 1))
                            mdone = vv.inc()
                            p.wait("tensor", vv, mdone)
                        else:
                            p.wait("tensor", sc, ex_done[r])
                        p.on("tensor", lambda e, prb=prb, kslc=kslc, r=r: e.matmul(
                            psC[:, :], vg[:, kslc:kslc + P], prb[:, :],
                            start=(r == 0), stop=(r == 8)))
                        p.on("tensor", lambda e, prb=prb, r=r: e.matmul(
                            psD[:, :], onesP[:, :], prb[:, :],
                            start=(r == 0), stop=(r == 8)).then_inc(tt.sem, 1))
                        av_done.append(tt.inc())
                    p.wait("vector", tt, tt.n)
                    p.on("vector", lambda e: e.reciprocal(rp1[:, :], psD[:, :]).then_inc(vv.sem, 1))
                    p.wait("vector", vv, vv.inc())
                    p.on("vector", lambda e: e.tensor_tensor(
                        attn_sb[:, :], psC[:, :], rp1[:, :], ALU.mult).then_inc(vv.sem, 1))
                    at_ok = vv.inc()
                    p.wait("tensor", vv, at_ok)
                    p.wait("sync", vv, at_ok)
                    for hh in range(4):
                        p.on("sync", lambda e, g=g, hh=hh, n=n: e.dma_start(
                            out=attnf_w[g * 4 + hh, :, n * P:(n + 1) * P],
                            in_=attn_sb[:, hh * P:(hh + 1) * P]).then_inc(st.sem, 16))
                        st.inc(16)
                    p.wait("vector", st, st.n)
            ATTN_DONE = st.n

            prev_use2 = prev_use
            for oc in range(8):
                mark = load_ws4(w0t_r, oc * 512, after=prev_use2)
                pr = Promise()
                prev_use2 = pr
                p.wait("tensor", gw, mark)
                for t in range(NT):
                    pb = psA if t % 2 == 0 else psB
                    for fc in range(FC):
                        last = fc == FC - 1
                        p.on("tensor", lambda e, fc=fc, t=t, pb=pb, last=last: (
                            e.matmul(pb[:, :], xnf_r[:, fc, t * P:(t + 1) * P],
                                     ws4_r[:, fc, :], start=(fc == 0),
                                     stop=last).then_inc(tt.sem, 1) if last else
                            e.matmul(pb[:, :], xnf_r[:, fc, t * P:(t + 1) * P],
                                     ws4_r[:, fc, :], start=(fc == 0), stop=last)))
                    mmd = tt.inc()
                    p.wait("vector", tt, mmd)
                    osl = slice(oc * 512, (oc + 1) * 512)

                    def _ep(e, pb=pb, t=t, osl=osl):
                        e.tensor_tensor(rp1[:, :], pb[:, :], bias_bc[:, osl], ALU.add)
                        e.engine_nop().then_inc(vv.sem, 1)
                    p.on("vector", _ep)
                    p.wait("vector", vv, vv.inc())

                    def _ep2(e, t=t, osl=osl):
                        e.tensor_scalar(out=rp2[:, :], in0=rp1[:, :],
                                        scalar1=qsel_sb[:, t:t + 1], scalar2=None,
                                        op0=ALU.mult)
                        e.engine_nop().then_inc(vv.sem, 1)
                    p.on("vector", _ep2)
                    p.wait("vector", vv, vv.inc())

                    def _ep3(e, t=t, osl=osl):
                        e.tensor_tensor(rp1[:, :], rp2[:, :], rowmask[:, osl], ALU.mult)
                        e.engine_nop().then_inc(vv.sem, 1)
                    p.on("vector", _ep3)
                    ep3 = vv.inc()
                    p.wait("tensor", vv, ep3)
                    p.wait("scalar", vv, ep3)
                    p.on("scalar", lambda e: e.activation(
                        out=gt[:, 0:512], in_=rp1[:, :], func=AF.Silu).then_inc(sc.sem, 1))
                    sl_ok = sc.inc()
                    p.wait("vector", sc, sl_ok)
                    p.wait("sync", sc, sl_ok)
                    p.on("sync", lambda e, t=t, oc=oc: e.dma_start(
                        out=gdr_r[t, :, oc * 512:(oc + 1) * 512],
                        in_=gt[:, 0:512]).then_inc(st.sem, 16))
                    st.inc(16)
                    p.wait("scalar", st, st.n)
                pr.val = tt.n
            G_DONE = st.n

            prev_use3 = prev_use2
            for oc in range(8):
                mark = load_ws4(wo_r, oc * 512, after=prev_use3)
                pr = Promise()
                prev_use3 = pr
                p.wait("tensor", gw, mark)
                if oc == 0:
                    p.wait("sync", st, G_DONE)
                for t in range(NT):
                    p.on("sync", lambda e, t=t: e.dma_start(
                        out=acol_r[:, :, :], in_=attnf_c[:, :, t * P:(t + 1) * P]
                    ).then_inc(sac.sem, 16))
                    sac.inc(16)
                    p.on("sync", lambda e, t=t, oc=oc: e.dma_start(
                        out=xt0[:, 0:512], in_=x0_r[t, :, oc * 512:(oc + 1) * 512]
                    ).then_inc(sxg.sem, 16))
                    sxg.inc(16)
                    p.on("sync", lambda e, t=t, oc=oc: e.dma_start(
                        out=gt[:, 512:1024], in_=gdr_r[t, :, oc * 512:(oc + 1) * 512]
                    ).then_inc(sxg.sem, 16))
                    sxg.inc(16)
                    pb = psC if t % 2 == 0 else psD
                    p.wait("tensor", sac, sac.n)
                    for ec in range(FC):
                        last = ec == FC - 1
                        p.on("tensor", lambda e, ec=ec, pb=pb, last=last: (
                            e.matmul(pb[:, :], acol_r[:, ec, :], ws4_r[:, ec, :],
                                     start=(ec == 0), stop=last).then_inc(tt.sem, 1)
                            if last else
                            e.matmul(pb[:, :], acol_r[:, ec, :], ws4_r[:, ec, :],
                                     start=(ec == 0), stop=last)))
                    mmw = tt.inc()
                    p.wait("vector", tt, mmw)
                    p.wait("vector", sxg, sxg.n)

                    def _cmb(e, pb=pb):
                        e.tensor_tensor(rp1[:, :], pb[:, :], gt[:, 512:1024], ALU.mult)
                        e.engine_nop().then_inc(vv.sem, 1)
                    p.on("vector", _cmb)
                    p.wait("vector", vv, vv.inc())

                    def _cmb2(e, pb=pb):
                        e.tensor_add(rp2[:, :], rp1[:, :], xt0[:, 0:512])
                        e.engine_nop().then_inc(vv.sem, 1)
                    p.on("vector", _cmb2)
                    cb = vv.inc()
                    p.wait("tensor", vv, cb)
                    p.wait("sync", vv, cb)
                    p.on("sync", lambda e, t=t, oc=oc: e.dma_start(
                        out=x1_w[t, :, oc * 512:(oc + 1) * 512],
                        in_=rp2[:, :]).then_inc(st.sem, 16))
                    st.inc(16)
                    p.wait("vector", st, st.n)
                    p.wait("sync", st, st.n)
                pr.val = tt.n
            X1_DONE = st.n

            tr_done2 = []
            for t in range(NT):
                p.wait("sync", st, X1_DONE if t == 0 else st.n)
                p.on("sync", lambda e, t=t: e.dma_start(
                    out=xt0[:, :], in_=x1_w[t, :, :]).then_inc(sx1.sem, 16))
                sx1.inc(16)
                p.wait("scalar", sx1, sx1.n)
                p.wait("vector", sx1, sx1.n)
                ln_ok = ln_tile(p, vv, sc, xt0, xnt0, qg, stats, t % 8)
                p.wait("tensor", vv, ln_ok)
                for q in range(FC // 4):
                    for j in range(4):
                        fc = q * 4 + j
                        p.on("tensor", lambda e, fc=fc, j=j: e.transpose(
                            ps_tr[:, j * P:(j + 1) * P], xnt0[:, fc * P:(fc + 1) * P],
                            idb[:, :]).then_inc(tt.sem, 1) if j == 3 else e.transpose(
                            ps_tr[:, j * P:(j + 1) * P], xnt0[:, fc * P:(fc + 1) * P], idb[:, :]))
                    qt = tt.inc()
                    p.wait("vector", tt, qt)

                    def _evac(e, q=q, t=t):
                        for j in range(4):
                            fc = q * 4 + j
                            e.tensor_copy(out=xnf_r[:, fc, t * P:(t + 1) * P],
                                          in_=ps_tr[:, j * P:(j + 1) * P])
                        e.engine_nop().then_inc(vv.sem, 1)
                    p.on("vector", _evac)
                    ev = vv.inc()
                    p.wait("tensor", vv, ev)
                tr_done2.append(tt.n)
                evn = vv.n
                p.wait("sync", vv, evn)
                for h in range(4):
                    sl = slice(h * 8, (h + 1) * 8)
                    p.on("sync", lambda e, t=t, sl=sl: e.dma_start(
                        out=xn1_d[:, sl, t * P:(t + 1) * P],
                        in_=xnf_r[:, sl, t * P:(t + 1) * P]).then_inc(st.sem, 16))
                    st.inc(16)
            XN1_ALL = vv.n

            prev_use4 = prev_use3
            for i in range(2 * NKV):
                g_, slot = divmod(i, 2)
                col = (g_ * 6 + 4 + slot) * P
                dst = k1_w if slot == 0 else v1_w
                mark = load_ws4(qkv_r, col, width=P, after=prev_use4)
                pr = Promise()
                prev_use4 = pr
                p.wait("tensor", gw, mark)
                if i == 0:
                    p.wait("tensor", vv, XN1_ALL)
                for t in range(NT):
                    pb = psA if t % 2 == 0 else psB
                    for fc in range(FC):
                        last = fc == FC - 1
                        p.on("tensor", lambda e, fc=fc, t=t, pb=pb, last=last: (
                            e.matmul(pb[:, 0:P], xnf_r[:, fc, t * P:(t + 1) * P],
                                     ws4_r[:, fc, 0:P], start=(fc == 0),
                                     stop=last).then_inc(tt.sem, 1) if last else
                            e.matmul(pb[:, 0:P], xnf_r[:, fc, t * P:(t + 1) * P],
                                     ws4_r[:, fc, 0:P], start=(fc == 0), stop=last)))
                    mmv = tt.inc()
                    p.wait("vector", tt, mmv)
                    p.on("vector", lambda e, pb=pb: e.tensor_copy(
                        out=vbf[:, :], in_=pb[:, 0:P]).then_inc(vv.sem, 1))
                    vd = vv.inc()
                    p.wait("tensor", vv, vd)
                    p.wait("sync", vv, vd)
                    p.on("sync", lambda e, t=t, g_=g_, dst=dst: e.dma_start(
                        out=dst[t, :, g_ * P:(g_ + 1) * P], in_=vbf[:, :]).then_inc(st.sem, 16))
                    st.inc(16)
                    p.wait("vector", st, st.n)
                pr.val = tt.n

            p.wait("sync", st, st.n)
            p.emit(blk)
    return nc


def build_launch3():
    nc = bass.Bass("TRN2", target_bir_lowering=False)
    x1i = nc.dram_tensor("x1i", [CHUNK, D], F32, kind="ExternalInput").ap()
    xn1i = nc.dram_tensor("xn1i", [D, CHUNK], BF16, kind="ExternalInput").ap()
    kown = nc.dram_tensor("kown", [CHUNK, NKV * HD], BF16, kind="ExternalInput").ap()
    vown = nc.dram_tensor("vown", [CHUNK, NKV * HD], BF16, kind="ExternalInput").ap()
    koth = nc.dram_tensor("koth", [CHUNK, NKV * HD], BF16, kind="ExternalInput").ap()
    voth = nc.dram_tensor("voth", [CHUNK, NKV * HD], BF16, kind="ExternalInput").ap()
    qkv1t = nc.dram_tensor("qkv1t", [D, 6144], F32, kind="ExternalInput").ap()
    p1t = nc.dram_tensor("p1t", [D, D], F32, kind="ExternalInput").ap()
    w1t = nc.dram_tensor("w1t", [D, D], F32, kind="ExternalInput").ap()
    bias1 = nc.dram_tensor("bias1", [1, D], F32, kind="ExternalInput").ap()
    g1t = nc.dram_tensor("g1t", [D, NEXP], F32, kind="ExternalInput").ap()
    insc = nc.dram_tensor("insc", [NEXP, D], F32, kind="ExternalInput").ap()
    outsc = nc.dram_tensor("outsc", [NEXP, D], F32, kind="ExternalInput").ap()
    madd = nc.dram_tensor("madd", [P, 1], F32, kind="ExternalInput").ap()
    ident = nc.dram_tensor("ident", [P, P], BF16, kind="ExternalInput").ap()

    x2_o = nc.dram_tensor("x2", [CHUNK, D], F32, kind="ExternalOutput").ap()
    q1f = nc.dram_tensor("q1f", [CHUNK, NQH * HD], BF16, kind="Internal").ap()
    attnf = nc.dram_tensor("attnf", [NQH * HD, CHUNK], BF16, kind="Internal").ap()
    gdr = nc.dram_tensor("gdr", [CHUNK, D], F32, kind="Internal").ap()

    es = ExitStack()
    with es:
        sb = lambda name, shape, dt=F32: es.enter_context(nc.sbuf_tensor(name, shape, dt))
        psum = lambda name, shape, dt=F32: es.enter_context(nc.psum_tensor(name, shape, dt))
        sem = lambda name: Ctr(es.enter_context(nc.semaphore(name)))

        xn_fm = sb("xn_fm", [P, FC * CHUNK], BF16)
        ws4 = sb("ws4", [P, FC * 512], BF16)
        qt_sb = sb("qt_sb", [P, D], BF16)
        kot = sb("kot", [P, NKV * HD], BF16)
        kxt = sb("kxt", [P, NKV * HD], BF16)
        vot = sb("vot", [P, NKV * HD], BF16)
        vxt = sb("vxt", [P, NKV * HD], BF16)
        qk_sc = sb("qk_sc", [P, P])
        scol = sb("scol", [P, 8 * NQH])
        attn_tok = sb("attn_tok", [P, D], BF16)
        probs_fm = sb("probs_fm", [NEXP, CHUNK], BF16)
        probs_tok = sb("probs_tok", [P, NEXP])
        probs_tbf = sb("probs_tbf", [P, NEXP], BF16)
        g1s = sb("g1s", [P, FC * NEXP], BF16)
        is_sb = sb("is_sb", [NEXP, D], BF16)
        os_sb = sb("os_sb", [NEXP, D], BF16)
        osb = sb("osb", [P, 512])
        acol = sb("acol", [P, FC * P], BF16)
        xt0 = sb("xt0", [P, 512])
        gf = sb("gf", [P, 1024])
        rp1 = sb("rp1", [P, 512])
        rp2 = sb("rp2", [P, 512])
        kbf = sb("kbf", [P, 512], BF16)
        idb = sb("idb", [P, P], BF16)
        brow_bf = sb("brow_bf", [1, D], BF16)
        ones1 = sb("ones1", [1, P], BF16)
        madd_c = sb("madd_c", [P, 1])
        selc = sb("selc", [P, 4])
        ps_tr = psum("ps_tr", [P, 512], BF16)
        psA = psum("psA", [P, 512])
        psB = psum("psB", [P, 512])
        psC = psum("psC", [P, 512])
        psD = psum("psD", [P, 512])

        cst = sem("s_cst")
        sxn = sem("s_sxn")
        sq_ = sem("s_q")
        skv = sem("s_skv")
        sac = sem("s_sac")
        sxg = sem("s_sxg")
        gw = sem("s_gw")
        gbx = sem("s_gbx")
        vv = sem("s_v")
        sc = sem("s_sc")
        tt = sem("s_t")
        st = sem("s_st")

        with nc.Block() as blk:
            p = PB(nc)
            xnf_r = xn_fm.rearrange("p (fc t) -> p fc t", t=CHUNK)
            ws4_r = ws4.rearrange("p (fc o) -> p fc o", o=512)
            xn1_d = xn1i.rearrange("(fc p) t -> p fc t", p=P)
            qkv_r = qkv1t.rearrange("(fc p) o -> p fc o", p=P)
            p1t_r = p1t.rearrange("(fc p) o -> p fc o", p=P)
            w1t_r = w1t.rearrange("(fc p) o -> p fc o", p=P)
            g1t_r = g1t.rearrange("(fc p) o -> p fc o", p=P)
            g1s_r = g1s.rearrange("p (fc o) -> p fc o", o=NEXP)
            q1f_r = q1f.rearrange("(n p) o -> n p o", p=P)
            kown_r = kown.rearrange("(n p) o -> n p o", p=P)
            vown_r = vown.rearrange("(n p) o -> n p o", p=P)
            koth_r = koth.rearrange("(n p) o -> n p o", p=P)
            voth_r = voth.rearrange("(n p) o -> n p o", p=P)
            attnf_w = attnf.rearrange("(ec p) t -> p ec t", p=P)
            attnf_c = attnf.rearrange("(ec p) t -> p ec t", p=P)
            acol_r = acol.rearrange("p (ec t) -> p ec t", t=P)
            at_r = attn_tok.rearrange("p (ec q) -> p ec q", q=P)
            x2_w = x2_o.rearrange("(n p) d -> n p d", p=P)
            x1_r = x1i.rearrange("(n p) d -> n p d", p=P)
            gdr_r = gdr.rearrange("(n p) d -> n p d", p=P)

            p.on("sync", lambda e: e.dma_start(out=idb[:, :], in_=ident[:, :]).then_inc(cst.sem, 16))
            cst.inc(16)
            p.on("sync", lambda e: e.dma_start(out=madd_c[:, :], in_=madd[:, :]).then_inc(cst.sem, 16))
            cst.inc(16)
            CONSTS = cst.n
            p.on("gpsimd", lambda e: e.dma_start(out=brow_bf[:, :],
                                                 in_=bias1[:, :]).then_inc(gbx.sem, 16))
            p.on("gpsimd", lambda e: e.dma_start(out=is_sb[:, :],
                                                 in_=insc[:, :]).then_inc(gbx.sem, 16))
            p.on("gpsimd", lambda e: e.dma_start(out=os_sb[:, :],
                                                 in_=outsc[:, :]).then_inc(gbx.sem, 16))
            p.on("gpsimd", lambda e: e.dma_start(out=g1s_r[:, :, :],
                                                 in_=g1t_r[:, :, :]).then_inc(gbx.sem, 16))
            gbx.inc(64)
            SMALLW = gbx.n
            p.on("vector", lambda e: e.memset(ones1[:, :], 1.0).then_inc(vv.sem, 1))
            ONES = vv.inc()
            for h in range(8):
                sl = slice(h * 4, (h + 1) * 4)
                p.on("sync", lambda e, sl=sl: e.dma_start(
                    out=xnf_r[:, sl, :], in_=xn1_d[:, sl, :]).then_inc(sxn.sem, 16))
                sxn.inc(16)
            XN_L = sxn.n

            bias_bc = sb
            bias_bc = es.enter_context(nc.sbuf_tensor("bias_bc", [P, D], BF16))
            p.wait("tensor", gbx, SMALLW)
            p.wait("tensor", vv, ONES)
            for oc in range(8):
                osl = slice(oc * 512, (oc + 1) * 512)
                p.on("tensor", lambda e, osl=osl: e.matmul(
                    psB[:, :], ones1[:, :], brow_bf[:, osl], start=True,
                    stop=True).then_inc(tt.sem, 1))
                tb_ = tt.inc()
                p.wait("vector", tt, tb_)
                p.on("vector", lambda e, osl=osl: e.tensor_copy(
                    out=bias_bc[:, osl], in_=psB[:, :]).then_inc(vv.sem, 1))
                p.wait("tensor", vv, vv.inc())

            gw_marks = []

            def load_ws4(src_r, col0, width=512, after=None):
                if after is not None:
                    p.wait("gpsimd", tt, after)
                for h in range(8):
                    sl = slice(h * 4, (h + 1) * 4)
                    p.on("gpsimd", lambda e, sl=sl, src_r=src_r, col0=col0, width=width: e.dma_start(
                        out=ws4_r[:, sl, 0:width],
                        in_=src_r[:, sl, col0:col0 + width]).then_inc(gw.sem, 16))
                    gw.inc(16)
                gw_marks.append(gw.n)
                return gw.n

            prev_use = None
            for g in range(NKV):
                mark = load_ws4(qkv_r, g * 6 * P, after=prev_use)
                pr = Promise()
                prev_use = pr
                p.wait("tensor", gw, mark)
                if g == 0:
                    p.wait("tensor", sxn, XN_L)
                for t in range(NT):
                    pb = psA if t % 2 == 0 else psB
                    for fc in range(FC):
                        last = fc == FC - 1
                        p.on("tensor", lambda e, fc=fc, t=t, pb=pb, last=last: (
                            e.matmul(pb[:, :], xnf_r[:, fc, t * P:(t + 1) * P],
                                     ws4_r[:, fc, :], start=(fc == 0),
                                     stop=last).then_inc(tt.sem, 1) if last else
                            e.matmul(pb[:, :], xnf_r[:, fc, t * P:(t + 1) * P],
                                     ws4_r[:, fc, :], start=(fc == 0), stop=last)))
                    mmq = tt.inc()
                    p.wait("vector", tt, mmq)
                    p.on("vector", lambda e, pb=pb: e.tensor_copy(
                        out=kbf[:, :], in_=pb[:, :]).then_inc(vv.sem, 1))
                    qd = vv.inc()
                    p.wait("tensor", vv, qd)
                    p.wait("sync", vv, qd)
                    p.on("sync", lambda e, t=t, g=g: e.dma_start(
                        out=q1f_r[t, :, g * 512:(g + 1) * 512], in_=kbf[:, :]).then_inc(st.sem, 16))
                    st.inc(16)
                    p.wait("vector", st, st.n)
                pr.val = tt.n
            Q_DONE = st.n

            p.wait("tensor", gbx, SMALLW)
            prev_e = None
            for t in range(NT):
                for fc in range(FC):
                    last = fc == FC - 1
                    p.on("tensor", lambda e, fc=fc, t=t, last=last: (
                        e.matmul(psC[:, 0:NEXP], xnf_r[:, fc, t * P:(t + 1) * P],
                                 g1s_r[:, fc, :], start=(fc == 0), stop=last
                                 ).then_inc(tt.sem, 1) if last else
                        e.matmul(psC[:, 0:NEXP], xnf_r[:, fc, t * P:(t + 1) * P],
                                 g1s_r[:, fc, :], start=(fc == 0), stop=last)))
                mmr = tt.inc()
                p.wait("scalar", tt, mmr)
                if prev_e is not None:
                    p.wait("scalar", vv, prev_e)
                p.on("scalar", lambda e: e.activation(
                    out=probs_tok[:, :], in_=psC[:, 0:NEXP], func=AF.Exp).then_inc(sc.sem, 1))
                e1 = sc.inc()
                p.wait("tensor", sc, e1)
                p.wait("vector", sc, e1)
                p.on("vector", lambda e: e.reduce_sum(selc[:, 0:1], probs_tok[:, :],
                                                      axis=AX.X).then_inc(vv.sem, 1))
                p.wait("vector", vv, vv.inc())
                p.on("vector", lambda e: e.reciprocal(selc[:, 0:1], selc[:, 0:1]).then_inc(vv.sem, 1))
                p.wait("vector", vv, vv.inc())
                p.on("vector", lambda e: e.tensor_scalar(
                    out=probs_tbf[:, :], in0=probs_tok[:, :], scalar1=selc[:, 0:1],
                    scalar2=None, op0=ALU.mult).then_inc(vv.sem, 1))
                pn = vv.inc()
                prev_e = pn
                p.wait("tensor", vv, pn)
                if t == 0:
                    p.wait("tensor", cst, CONSTS)
                p.on("tensor", lambda e: e.transpose(
                    ps_tr[0:NEXP, 0:P], probs_tbf[:, :], idb[:, :]).then_inc(tt.sem, 1))
                trk = tt.inc()
                p.wait("vector", tt, trk)
                p.on("vector", lambda e, t=t: e.tensor_copy(
                    out=probs_fm[:, t * P:(t + 1) * P], in_=ps_tr[0:NEXP, 0:P]).then_inc(vv.sem, 1))
                pfm = vv.inc()
                p.wait("tensor", vv, pfm)
            PFM_DONE = vv.n

            p.wait("tensor", vv, PFM_DONE)
            for fc in range(FC):
                for tb in range(2):
                    pb = psA if tb == 0 else psB
                    p.on("tensor", lambda e, fc=fc, tb=tb, pb=pb: e.matmul(
                        pb[:, :], is_sb[:, fc * P:(fc + 1) * P],
                        probs_fm[:, tb * 512:(tb + 1) * 512],
                        start=True, stop=True).then_inc(tt.sem, 1))
                    mi = tt.inc()
                    p.wait("vector", tt, mi)
                    p.on("vector", lambda e, fc=fc, tb=tb, pb=pb: e.tensor_tensor(
                        xnf_r[:, fc, tb * 512:(tb + 1) * 512], pb[:, :],
                        xnf_r[:, fc, tb * 512:(tb + 1) * 512], ALU.mult).then_inc(vv.sem, 1))
                    xs_ok = vv.inc()
                    p.wait("tensor", vv, xs_ok)
            XS_DONE = vv.n

            p.wait("sync", st, Q_DONE)
            p.wait("vector", cst, CONSTS)
            OS, XS_, MM, EO, EX, DEN, PO, PX = (slice(i * NQH, (i + 1) * NQH) for i in range(8))
            for t in range(NT):
                p.on("sync", lambda e, t=t: e.dma_start(
                    out=qt_sb[:, :], in_=q1f_r[t, :, :]).then_inc(sq_.sem, 16))
                sq_.inc(16)
                for dst, srcr in ((kot, kown_r), (kxt, koth_r), (vot, vown_r), (vxt, voth_r)):
                    p.on("sync", lambda e, dst=dst, srcr=srcr, t=t: e.dma_start(
                        out=dst[:, :], in_=srcr[t, :, :]).then_inc(skv.sem, 16))
                    skv.inc(16)
                p.wait("vector", sq_, sq_.n)
                p.wait("vector", skv, skv.n)
                for h in range(NQH):
                    g = h // GQ
                    hs = slice(h * HD, (h + 1) * HD)
                    gs = slice(g * HD, (g + 1) * HD)
                    p.on("vector", lambda e, hs=hs, gs=gs: e.tensor_tensor(
                        qk_sc[:, :], qt_sb[:, hs], kot[:, gs], ALU.mult).then_inc(vv.sem, 1))
                    p.wait("vector", vv, vv.inc())
                    p.on("vector", lambda e, h=h: e.reduce_sum(
                        scol[:, OS][:, h:h + 1], qk_sc[:, :], axis=AX.X).then_inc(vv.sem, 1))
                    p.wait("vector", vv, vv.inc())
                    p.on("vector", lambda e, hs=hs, gs=gs: e.tensor_tensor(
                        qk_sc[:, :], qt_sb[:, hs], kxt[:, gs], ALU.mult).then_inc(vv.sem, 1))
                    p.wait("vector", vv, vv.inc())
                    p.on("vector", lambda e, h=h: e.reduce_sum(
                        scol[:, XS_][:, h:h + 1], qk_sc[:, :], axis=AX.X).then_inc(vv.sem, 1))
                    p.wait("vector", vv, vv.inc())
                p.on("vector", lambda e: e.tensor_scalar(
                    out=scol[:, OS], in0=scol[:, OS], scalar1=SMS, scalar2=None,
                    op0=ALU.mult).then_inc(vv.sem, 1))
                p.wait("vector", vv, vv.inc())
                p.on("vector", lambda e: e.tensor_scalar(
                    out=scol[:, XS_], in0=scol[:, XS_], scalar1=SMS, scalar2=madd_c[:, :],
                    op0=ALU.mult, op1=ALU.add).then_inc(vv.sem, 1))
                p.wait("vector", vv, vv.inc())
                p.on("vector", lambda e: e.tensor_tensor(
                    scol[:, MM], scol[:, OS], scol[:, XS_], ALU.max).then_inc(vv.sem, 1))
                p.wait("vector", vv, vv.inc())
                p.on("vector", lambda e: e.tensor_tensor(
                    scol[:, EO], scol[:, OS], scol[:, MM], ALU.subtract).then_inc(vv.sem, 1))
                p.wait("vector", vv, vv.inc())
                p.on("vector", lambda e: e.tensor_tensor(
                    scol[:, EX], scol[:, XS_], scol[:, MM], ALU.subtract).then_inc(vv.sem, 1))
                sme = vv.inc()
                p.wait("vector", vv, sme)
                p.wait("scalar", vv, sme)
                p.on("scalar", lambda e: e.activation(
                    out=scol[:, EO], in_=scol[:, EO], func=AF.Exp).then_inc(sc.sem, 1))
                p.wait("scalar", sc, sc.inc())
                p.on("scalar", lambda e: e.activation(
                    out=scol[:, EX], in_=scol[:, EX], func=AF.Exp).then_inc(sc.sem, 1))
                exd = sc.inc()
                p.wait("vector", sc, exd)
                p.on("vector", lambda e: e.tensor_tensor(
                    scol[:, DEN], scol[:, EO], scol[:, EX], ALU.add).then_inc(vv.sem, 1))
                p.wait("vector", vv, vv.inc())
                p.on("vector", lambda e: e.reciprocal(scol[:, DEN], scol[:, DEN]).then_inc(vv.sem, 1))
                p.wait("vector", vv, vv.inc())
                p.on("vector", lambda e: e.tensor_tensor(
                    scol[:, PO], scol[:, EO], scol[:, DEN], ALU.mult).then_inc(vv.sem, 1))
                p.wait("vector", vv, vv.inc())
                p.on("vector", lambda e: e.tensor_tensor(
                    scol[:, PX], scol[:, EX], scol[:, DEN], ALU.mult).then_inc(vv.sem, 1))
                p.wait("vector", vv, vv.inc())
                for h in range(NQH):
                    g = h // GQ
                    hs = slice(h * HD, (h + 1) * HD)
                    gs = slice(g * HD, (g + 1) * HD)
                    p.on("vector", lambda e, hs=hs, gs=gs, h=h: e.tensor_scalar(
                        out=qk_sc[:, :], in0=vot[:, gs], scalar1=scol[:, PO][:, h:h + 1],
                        scalar2=None, op0=ALU.mult).then_inc(vv.sem, 1))
                    p.wait("vector", vv, vv.inc())
                    p.on("vector", lambda e, hs=hs, gs=gs, h=h: e.tensor_scalar(
                        out=attn_tok[:, hs], in0=vxt[:, gs], scalar1=scol[:, PX][:, h:h + 1],
                        scalar2=None, op0=ALU.mult).then_inc(vv.sem, 1))
                    p.wait("vector", vv, vv.inc())
                    p.on("vector", lambda e, hs=hs: e.tensor_tensor(
                        attn_tok[:, hs], attn_tok[:, hs], qk_sc[:, :], ALU.add).then_inc(vv.sem, 1))
                    p.wait("vector", vv, vv.inc())
                AT_OK = vv.n
                p.wait("tensor", vv, AT_OK)
                for q in range(FC // 4):
                    for j in range(4):
                        ec = q * 4 + j
                        p.on("tensor", lambda e, ec=ec, j=j: e.transpose(
                            ps_tr[:, j * P:(j + 1) * P], attn_tok[:, ec * P:(ec + 1) * P],
                            idb[:, :]).then_inc(tt.sem, 1) if j == 3 else e.transpose(
                            ps_tr[:, j * P:(j + 1) * P], attn_tok[:, ec * P:(ec + 1) * P], idb[:, :]))
                    qt2 = tt.inc()
                    p.wait("vector", tt, qt2)

                    def _evac(e, q=q, t=t):
                        for j in range(4):
                            ec = q * 4 + j
                            e.tensor_copy(out=acol_r[:, ec, :], in_=ps_tr[:, j * P:(j + 1) * P])
                        e.engine_nop().then_inc(vv.sem, 1)
                    p.on("vector", _evac)
                    ev = vv.inc()
                    p.wait("tensor", vv, ev)
                evn = vv.n
                p.wait("sync", vv, evn)
                for hq in range(4):
                    sl = slice(hq * 8, (hq + 1) * 8)
                    p.on("sync", lambda e, t=t, sl=sl: e.dma_start(
                        out=attnf_w[:, sl, t * P:(t + 1) * P],
                        in_=acol_r[:, sl, :]).then_inc(st.sem, 16))
                    st.inc(16)
                p.wait("sync", st, st.n)
                p.wait("vector", st, st.n)
            ATTN_DONE = st.n

            prev_use2 = prev_use
            for oc in range(8):
                mark = load_ws4(w1t_r, oc * 512, after=prev_use2)
                pr = Promise()
                prev_use2 = pr
                p.wait("scalar", gw, mark)
                p.on("scalar", lambda e: e.activation(
                    out=ws4[:, :], in_=ws4[:, :], func=AF.Sign).then_inc(sc.sem, 1))
                sgn = sc.inc()
                p.wait("tensor", sc, sgn)
                if oc == 0:
                    p.wait("tensor", vv, XS_DONE)
                for t in range(NT):
                    pb = psA if t % 2 == 0 else psB
                    for fc in range(FC):
                        last = fc == FC - 1
                        p.on("tensor", lambda e, fc=fc, t=t, pb=pb, last=last: (
                            e.matmul(pb[:, :], xnf_r[:, fc, t * P:(t + 1) * P],
                                     ws4_r[:, fc, :], start=(fc == 0),
                                     stop=last).then_inc(tt.sem, 1) if last else
                            e.matmul(pb[:, :], xnf_r[:, fc, t * P:(t + 1) * P],
                                     ws4_r[:, fc, :], start=(fc == 0), stop=last)))
                    mmd = tt.inc()
                    p.on("tensor", lambda e, t=t, oc=oc: e.matmul(
                        psD[:, :], probs_fm[:, t * P:(t + 1) * P],
                        os_sb[:, oc * 512:(oc + 1) * 512],
                        start=True, stop=True).then_inc(tt.sem, 1))
                    mos = tt.inc()
                    p.wait("vector", tt, mos)
                    p.on("vector", lambda e: e.tensor_copy(out=osb[:, :],
                                                           in_=psD[:, :]).then_inc(vv.sem, 1))
                    ob = vv.inc()
                    p.wait("vector", vv, ob)
                    osl = slice(oc * 512, (oc + 1) * 512)

                    def _ep(e, pb=pb):
                        e.tensor_tensor(rp1[:, :], pb[:, :], osb[:, :], ALU.mult)
                        e.engine_nop().then_inc(vv.sem, 1)
                    p.on("vector", _ep)
                    p.wait("vector", vv, vv.inc())

                    def _ep2(e, osl=osl):
                        e.tensor_tensor(rp2[:, :], rp1[:, :], bias_bc[:, osl], ALU.add)
                        e.engine_nop().then_inc(vv.sem, 1)
                    p.on("vector", _ep2)
                    ep2 = vv.inc()
                    p.wait("tensor", vv, ep2)
                    p.wait("scalar", vv, ep2)
                    p.on("scalar", lambda e: e.activation(
                        out=gf[:, 0:512], in_=rp2[:, :], func=AF.Silu).then_inc(sc.sem, 1))
                    sl_ok = sc.inc()
                    p.wait("vector", sc, sl_ok)
                    p.wait("sync", sc, sl_ok)
                    p.on("sync", lambda e, t=t, oc=oc: e.dma_start(
                        out=gdr_r[t, :, oc * 512:(oc + 1) * 512],
                        in_=gf[:, 0:512]).then_inc(st.sem, 16))
                    st.inc(16)
                    p.wait("scalar", st, st.n)
                pr.val = tt.n
            G_DONE = st.n

            prev_use3 = prev_use2
            for oc in range(8):
                mark = load_ws4(p1t_r, oc * 512, after=prev_use3)
                pr = Promise()
                prev_use3 = pr
                p.wait("tensor", gw, mark)
                if oc == 0:
                    p.wait("sync", st, G_DONE)
                for t in range(NT):
                    p.on("sync", lambda e, t=t: e.dma_start(
                        out=acol_r[:, :, :], in_=attnf_c[:, :, t * P:(t + 1) * P]
                    ).then_inc(sac.sem, 16))
                    sac.inc(16)
                    p.on("sync", lambda e, t=t, oc=oc: e.dma_start(
                        out=xt0[:, :], in_=x1_r[t, :, oc * 512:(oc + 1) * 512]
                    ).then_inc(sxg.sem, 16))
                    sxg.inc(16)
                    p.on("sync", lambda e, t=t, oc=oc: e.dma_start(
                        out=gf[:, 512:1024], in_=gdr_r[t, :, oc * 512:(oc + 1) * 512]
                    ).then_inc(sxg.sem, 16))
                    sxg.inc(16)
                    pb = psC if t % 2 == 0 else psD
                    p.wait("tensor", sac, sac.n)
                    for ec in range(FC):
                        last = ec == FC - 1
                        p.on("tensor", lambda e, ec=ec, pb=pb, last=last: (
                            e.matmul(pb[:, :], acol_r[:, ec, :], ws4_r[:, ec, :],
                                     start=(ec == 0), stop=last).then_inc(tt.sem, 1)
                            if last else
                            e.matmul(pb[:, :], acol_r[:, ec, :], ws4_r[:, ec, :],
                                     start=(ec == 0), stop=last)))
                    mmw = tt.inc()
                    p.wait("vector", tt, mmw)
                    p.wait("vector", sxg, sxg.n)

                    def _cmb(e, pb=pb):
                        e.tensor_tensor(rp1[:, :], pb[:, :], gf[:, 512:1024], ALU.mult)
                        e.engine_nop().then_inc(vv.sem, 1)
                    p.on("vector", _cmb)
                    p.wait("vector", vv, vv.inc())

                    def _cmb2(e, pb=pb):
                        e.tensor_add(rp2[:, :], rp1[:, :], xt0[:, :])
                        e.engine_nop().then_inc(vv.sem, 1)
                    p.on("vector", _cmb2)
                    cb = vv.inc()
                    p.wait("tensor", vv, cb)
                    p.wait("sync", vv, cb)
                    p.on("sync", lambda e, t=t, oc=oc: e.dma_start(
                        out=x2_w[t, :, oc * 512:(oc + 1) * 512],
                        in_=rp2[:, :]).then_inc(st.sem, 16))
                    st.inc(16)
                    p.wait("vector", st, st.n)
                    p.wait("sync", st, st.n)
                pr.val = tt.n

            p.wait("sync", st, st.n)
            p.emit(blk)
    return nc



def _rope_tables(positions):
    half = HD // 2
    inv_freq = 10000.0 ** (-np.arange(half, dtype=np.float64) / half)
    ang = positions[None, :].astype(np.float64) * inv_freq[:, None]
    return np.ascontiguousarray(np.cos(ang).astype(np.float32)), \
        np.ascontiguousarray(np.sin(ang).astype(np.float32))


def _t(a):
    return np.ascontiguousarray(np.asarray(a).T)


def launch1_inputs(inputs):
    x0 = np.asarray(inputs["L0_attn_x"], np.float32)
    ct = _t(inputs["L0_ffn_centroids"])
    wk = np.ascontiguousarray(np.asarray(inputs["L0_attn_wk"], np.float32))
    wv = np.ascontiguousarray(np.asarray(inputs["L0_attn_wv"], np.float32))
    ident = np.eye(P, dtype=bfdt)
    maps = []
    for core in range(NCORES):
        b, c = divmod(core, 4)
        pos = np.arange(c * CHUNK, (c + 1) * CHUNK)
        cosb, sinb = _rope_tables(pos)
        maps.append({
            "x0": np.ascontiguousarray(x0[b, c * CHUNK:(c + 1) * CHUNK, :]),
            "ct": ct, "wk": wk, "wv": wv,
            "cosb": cosb, "sinb": sinb, "ident": ident,
        })
    return maps


def _l0_masks(is_chunk0):
    kki = np.arange(P)[:, None]
    qqi = np.arange(P)[None, :]
    blk0 = (qqi < kki).astype(np.float32)
    blk8 = (qqi >= kki).astype(np.float32)
    m0 = np.tile(blk0, (1, 4))
    m8 = np.tile(blk8, (1, 4))
    mc = np.ones((P, 512), np.float32)
    if is_chunk0:
        m0[:] = 0.0
        mc[:] = 0.0
    return m0.astype(bfdt), m8.astype(bfdt), mc.astype(bfdt)


def launch2_inputs(inputs, r1):
    x0 = np.asarray(inputs["L0_attn_x"], np.float32)
    wq = np.ascontiguousarray(np.asarray(inputs["L0_attn_wq"], np.float32))
    wo = np.ascontiguousarray(np.asarray(inputs["L0_attn_wo"], np.float32))
    w0t = _t(inputs["L0_ffn_weight"])
    qkv1t = _t(inputs["L1_attn_weight_qkv"])
    bias0 = np.ascontiguousarray(np.asarray(inputs["L0_ffn_bias"], np.float32)[None, :])
    asg = np.ascontiguousarray(np.asarray(inputs["L0_ffn_assignments"], np.int32)[None, :])
    cnts = np.ascontiguousarray(np.stack([r1[c]["counts"][0] for c in range(NCORES)]))
    iota64 = np.arange(P, dtype=np.float32)[:, None].copy()
    ident = np.eye(P, dtype=bfdt)
    zk = np.zeros((NKV * HD, CHUNK), bfdt)
    zv = np.zeros((CHUNK, NKV * HD), bfdt)
    maps = []
    for core in range(NCORES):
        b, c = divmod(core, 4)
        kprev = r1[core - 1]["k0"] if c > 0 else zk
        vprev = r1[core - 1]["v0"] if c > 0 else zv
        kkc = np.ascontiguousarray(np.concatenate([kprev, r1[core]["k0"]], axis=1))
        vvc = np.ascontiguousarray(np.concatenate([vprev, r1[core]["v0"]], axis=0))
        pos = np.arange(c * CHUNK, (c + 1) * CHUNK)
        cosb, sinb = _rope_tables(pos)
        m0, m8, mc = _l0_masks(c == 0)
        maps.append({
            "x0": np.ascontiguousarray(x0[b, c * CHUNK:(c + 1) * CHUNK, :]),
            "xn0": r1[core]["xn0"], "kk": kkc, "vvt": vvc,
            "cnts": cnts, "qsel": r1[core]["qsel"], "asg": asg,
            "wq": wq, "wo": wo, "w0t": w0t, "qkv1t": qkv1t, "bias0": bias0,
            "cosb": cosb, "sinb": sinb, "mask0": m0, "mask8": m8, "maskc": mc,
            "iota64": iota64, "ident": ident,
        })
    return maps


def launch3_inputs(inputs, r2):
    qkv1t = _t(inputs["L1_attn_weight_qkv"])
    p1t = _t(inputs["L1_attn_weight_proj"])
    w1t = _t(inputs["L1_ffn_weight"])
    g1t = _t(inputs["L1_ffn_gate_w"])
    bias1 = np.ascontiguousarray(np.asarray(inputs["L1_ffn_bias"], np.float32)[None, :])
    insc = np.ascontiguousarray(np.asarray(inputs["L1_ffn_in_channel_scale"], np.float32))
    outsc = np.ascontiguousarray(np.asarray(inputs["L1_ffn_out_channel_scale"], np.float32))
    ident = np.eye(P, dtype=bfdt)
    maps = []
    for core in range(NCORES):
        b, c = divmod(core, 4)
        other = (1 - b) * 4 + c
        madd = np.full((P, 1), -1e30 if b == 0 else 0.0, np.float32)
        maps.append({
            "x1i": r2[core]["x1"], "xn1i": r2[core]["xn1"],
            "kown": r2[core]["k1"], "vown": r2[core]["v1"],
            "koth": r2[other]["k1"], "voth": r2[other]["v1"],
            "qkv1t": qkv1t, "p1t": p1t, "w1t": w1t, "bias1": bias1,
            "g1t": g1t, "insc": insc, "outsc": outsc,
            "madd": madd, "ident": ident,
        })
    return maps


def kernel(**inputs):
    nc1 = build_launch1()
    r1 = [dict(r) for r in run_bass_kernel_spmd(
        nc1, launch1_inputs(inputs), core_ids=list(range(NCORES))).results]
    nc2 = build_launch2()
    r2 = [dict(r) for r in run_bass_kernel_spmd(
        nc2, launch2_inputs(inputs, r1), core_ids=list(range(NCORES))).results]
    nc3 = build_launch3()
    r3 = [dict(r) for r in run_bass_kernel_spmd(
        nc3, launch3_inputs(inputs, r2), core_ids=list(range(NCORES))).results]
    out = np.empty((B, S, D), np.float32)
    for core in range(NCORES):
        b, c = divmod(core, 4)
        out[b, c * CHUNK:(c + 1) * CHUNK, :] = r3[core]["x2"]
    return out
